# version 13
# baseline (speedup 1.0000x reference)
import numpy as np
from contextlib import ExitStack

import concourse.bass as bass
import concourse.mybir as mybir
from concourse.bass_utils import run_bass_kernel_spmd

F32 = mybir.dt.float32
F32R = mybir.dt.float32r
F16 = mybir.dt.float16
F8 = mybir.dt.float8e4
BF16 = mybir.dt.bfloat16
AF = mybir.ActivationFunctionType
ALU = mybir.AluOpType

NCORES = 8
B = 2
S = 2048
D = 1024
NH = 16
HD = 64
RD = 32
NBLK = B * 4
EPS = 1e-5
DEBUG = False


class Sem:
    def __init__(self, nc, es, name):
        self.h = es.enter_context(nc.semaphore(name))
        self.n = 0

    def inc(self, inst, k=1):
        inst.then_inc(self.h, k)
        self.n += k
        return self.n


class Kb:

    def __init__(self):
        self.waited = {}
        self.res = {}

    def wt(self, eng, ename, sem, val):
        if val is None or val <= 0:
            return
        key = (ename, id(sem))
        if self.waited.get(key, 0) < val:
            eng.wait_ge(sem.h, val)
            self.waited[key] = val

    def rwait(self, eng, ename, key):
        e = self.res.get(key)
        if e is not None:
            self.wt(eng, ename, e[0], e[1])

    def rset(self, key, sem, val):
        self.res[key] = (sem, val)


def build_nc():
    nc = bass.Bass("TRN2", target_bir_lowering=False)
    ap = lambda t: t.ap()
    x_in = ap(nc.dram_tensor("x_in", [B, S, D], F32, kind="ExternalInput"))
    xT_in = ap(nc.dram_tensor("xT_in", [B, D, S], F32, kind="ExternalInput"))
    wqkT_in = ap(nc.dram_tensor("wqkT_in", [D, 2 * D], F32, kind="ExternalInput"))
    wvT_in = ap(nc.dram_tensor("wvT_in", [D, D], F32, kind="ExternalInput"))
    woutT_in = ap(nc.dram_tensor("woutT_in", [D, D], F32, kind="ExternalInput"))
    wrT_in = ap(nc.dram_tensor("wrT_in", [D, D], F32R, kind="ExternalInput"))
    wiT_in = ap(nc.dram_tensor("wiT_in", [D, D], F32R, kind="ExternalInput"))
    cdft_in = ap(nc.dram_tensor("cdft_in", [D, D], F32R, kind="ExternalInput"))
    sdft_in = ap(nc.dram_tensor("sdft_in", [D, D], F32R, kind="ExternalInput"))
    sneg_in = ap(nc.dram_tensor("sneg_in", [D, D], F32R, kind="ExternalInput"))
    rcos_in = ap(nc.dram_tensor("rcos_in", [128, S], F32, kind="ExternalInput"))
    rsin_in = ap(nc.dram_tensor("rsin_in", [128, S], F32, kind="ExternalInput"))
    tmat_in = ap(nc.dram_tensor("tmat_in", [128, 128], F32, kind="ExternalInput"))
    mask_in = ap(nc.dram_tensor("mask_in", [128, 128], F32, kind="ExternalInput"))
    idf_in = ap(nc.dram_tensor("idf_in", [128, 128], F32, kind="ExternalInput"))
    bias_in = ap(nc.dram_tensor("bias_in", [1, D], F32, kind="ExternalInput"))
    const_in = ap(nc.dram_tensor("const_in", [128, 640], F32, kind="ExternalInput"))
    mhbm = ap(nc.dram_tensor("mhbm", [D, D], F16,
                             kind="ExternalOutput" if DEBUG else "Internal"))
    wvbf = ap(nc.dram_tensor("wvbf", [D, D], F16, kind="Internal"))
    woutbf = ap(nc.dram_tensor("woutbf", [D, D], F16, kind="Internal"))
    y_out = ap(nc.dram_tensor("y_out", [B, S, D], F32, kind="ExternalOutput"))

    r3 = lambda a, p=128: a.rearrange("(a p) c -> p a c", p=p)
    kb = Kb()
    rec = {}

    es = ExitStack()
    with es:
        sb = lambda name, shape, dt: es.enter_context(nc.sbuf_tensor(name, shape, dt))
        wqk = sb("wqk", [128, 8 * 2048], F16)
        rcs = [sb(f"rcs{i}", [128, 512], BF16) for i in range(2)]
        rsn = [sb(f"rsn{i}", [128, 512], BF16) for i in range(2)]
        tmat = sb("tmat", [128, 128], BF16)
        maskt = sb("maskt", [128, 128], BF16)
        idf = sb("idf", [128, 128], F32)
        biasr = sb("biasr", [1, D], F16)
        ones64 = sb("ones64", [128, 64], BF16)
        ones1r = sb("ones1r", [1, 128], F16)
        ones4r = sb("ones4r", [4, 128], F32R)
        onese_h = sb("onese_h", [128, 4], F16)
        cwqk4 = sb("cwqk4", [4, 2 * D], F16)
        cwv4 = sb("cwv4", [4, D], F16)
        cm4 = sb("cm4", [4, D], F16)
        s1c = sb("s1c", [128, 32], F32)
        s2c = sb("s2c", [128, 32], F32)
        a_sb = sb("a_sb", [128, 32], F32)
        negm = sb("negm", [128, 32], F32)
        vpe = sb("vpe", [128, 32], F32)
        tst1 = sb("tst1", [128, 32], F32)
        tst2 = sb("tst2", [128, 32], F32)
        tst3 = sb("tst3", [128, 32], F32)
        a4r = [sb(f"a4r{i}", [4, 512], F32R) for i in range(2)]
        nm4r = [sb(f"nm4r{i}", [4, 512], F16) for i in range(2)]
        nm4b = [sb(f"nm4b{i}", [4, 512], BF16) for i in range(2)]
        stga_all = sb("stga_all", [32, 128], F32)
        stgn_all = sb("stgn_all", [32, 128], F32)

        sm = lambda name: Sem(nc, es, name)
        sCst = sm("sCst")
        sChk = sm("sChk")
        sWsl = [sm("sWslA"), sm("sWslB")]
        sCt = [sm("sCtA"), sm("sCtB")]
        sWt = sm("sWt")
        sBnc = [sm("sBncA"), sm("sBncB")]
        sXtb = [sm(f"sXtb{i}") for i in range(2)]
        sXtr = [sm(f"sXtr{i}") for i in range(2)]
        sA4 = [sm(f"sA4{i}") for i in range(2)]
        sIdf = sm("sIdf")
        sXst = [sm("sXstA"), sm("sXstB")]
        sXr = [sm(f"sXr{i}") for i in range(4)]
        sVs = [sm(f"sVs{i}") for i in range(6)]
        sWs = [sm(f"sWs{i}") for i in range(6)]
        sMs = [sm(f"sMs{i}") for i in range(6)]
        tP0 = sm("tP0")
        vP0 = sm("vP0")
        cSq = sm("cSq")
        vRd = sm("vRd")
        cSq2 = sm("cSq2")
        vSt = sm("vSt")
        tB = sm("tB")
        vB = sm("vB")
        cB = sm("cB")
        oST = [sm(f"oST{i}") for i in range(4)]
        oM = [sm("oMA"), sm("oMB")]

        def dma(sem, eng, out, in_):
            eng.dma_start(out=out, in_=in_).then_inc(sem.h, 16)
            sem.n += 16
            return (sem, sem.n)

        blk = es.enter_context(nc.Block())

        tp = {}
        n = 0
        for jc in range(2):
            for kp in range(8):
                n += 1; tp[("wgr", jc, kp)] = n
                n += 1; tp[("wgi", jc, kp)] = n
            for it in range(8):
                n += 1; tp[("m", jc, it)] = n
                if it >= 1:
                    n += 1; tp[("cmm", jc, it - 1)] = n
            n += 1; tp[("cmm", jc, 7)] = n
        for k in range(4):
            n += 1; tp[("cwvg", k)] = n
        for c in range(4):
            n += 1; tp[("cwqk", c)] = n
        tp["end"] = n
        vp = {}
        n = 0
        for jc in range(2):
            for kp in range(8):
                n += 1; vp[("ewgr", jc, kp)] = n
                n += 1; vp[("ewgi", jc, kp)] = n
            if jc == 1:
                n += 8
            for it in range(8):
                n += 1; vp[("em", jc, it)] = n
            n += 1; vp[("ecm", jc)] = n
        for k in range(4):
            n += 1; vp[("ecwv", k)] = n
        for c in range(4):
            n += 1; vp[("ecwqk", c)] = n
        vp["end"] = n

        with ExitStack() as es0:
            sb0 = lambda name, shape, dt: es0.enter_context(nc.sbuf_tensor(name, shape, dt))
            cC = sb0("cC", [128, 8 * 512], F32R)
            cS_ = sb0("cS_", [128, 8 * 512], F32R)
            cN = sb0("cN", [128, 8 * 512], F32R)
            wsl_r = [sb0(f"wsl_r{i}", [128, 1024], F32R) for i in range(2)]
            wsl_i = [sb0(f"wsl_i{i}", [128, 1024], F32R) for i in range(2)]
            cstC = [sb0(f"cstC{i}", [128, 1024], F32R) for i in range(2)]
            cstS = [sb0(f"cstS{i}", [128, 1024], F32R) for i in range(2)]
            wg_r = sb0("wg_r", [128, 8 * 512], F32R)
            wg_i = sb0("wg_i", [128, 8 * 512], F32R)
            mst = [sb0(f"mst{i}", [128, 512], F16) for i in range(2)]
            bnc = sb0("bnc", [128, 4096], F16)
            xst = [sb0(f"xst{i}", [128, 1024], F32) for i in range(2)]
            ps0 = lambda name, shape: es0.enter_context(nc.psum_tensor(name, shape, F32))
            pwgr = [ps0(f"pwgr{i}", [128, 512]) for i in range(2)]
            pwgi = [ps0(f"pwgi{i}", [128, 512]) for i in range(2)]
            pm = [ps0(f"pm{i}", [128, 512]) for i in range(2)]
            pcm = ps0("pcm", [4, 512])
            pcw = ps0("pcw", [4, 512])

            def g0(g):
                dma(sCst, g, ones64[:], const_in[:, 0:64])
                dma(sCst, g, ones1r[:], const_in[0:1, 0:128])
                dma(sCst, g, ones4r[:], const_in[0:4, 0:128])
                dma(sCst, g, onese_h[:], const_in[:, 0:4])
                for i in range(2):
                    dma(sCst, g, a4r[i][:], const_in[0:4, 128:640])
                    dma(sCst, g, nm4r[i][:], const_in[0:4, 128:640])
                    rec["consts"] = dma(sCst, g, nm4b[i][:], const_in[0:4, 128:640])
                dma(sWt, g, wqk[:].rearrange("p (a f) -> p a f", a=8), r3(wqkT_in))
                dma(sWt, g, tmat[:], tmat_in)
                dma(sWt, g, maskt[:], mask_in)
                rec["weights"] = dma(sWt, g, biasr[:], bias_in)
                nb = [0]

                def bounce(srcw, dst, r):
                    if nb[0] >= 1:
                        kb.wt(g, "g", *rec[("bnc2", nb[0] - 1)])
                    if nb[0] == 2:
                        kb.wt(g, "g", tP0, tp[("cwvg", 3)])
                    rec[("bnc1", nb[0])] = dma(
                        sBnc[0], g, bnc[:].rearrange("p (a c) -> p a c", a=4),
                        r3(srcw[r * 512:(r + 1) * 512, :]))
                    kb.wt(g, "g", *rec[("bnc1", nb[0])])
                    rec[("bnc2", nb[0])] = dma(
                        sBnc[0], g,
                        r3(dst[r * 512:(r + 1) * 512, :]),
                        bnc[:].rearrange("p (a c) -> p a c", a=4))
                    nb[0] += 1

                def mdmas(jc):
                    for it in range(8):
                        kb.wt(g, "g", vP0, vp[("em", jc, it)])
                        rec[("mdma", jc, it)] = dma(
                            oM[it % 2], g,
                            mhbm[it * 128:(it + 1) * 128, jc * 512:(jc + 1) * 512],
                            mst[it % 2][:])

                bounce(wvT_in, wvbf, 0)
                bounce(wvT_in, wvbf, 1)
                mdmas(0)
                mdmas(1)
                bounce(woutT_in, woutbf, 0)
                bounce(woutT_in, woutbf, 1)
                rec["wbf_a"] = (sBnc[0], sBnc[0].n)
                rec["wbf_b"] = (sBnc[1], sBnc[1].n)

            blk.gpsimd(g0)

            def s0(s):
                rec["idf"] = dma(sIdf, s, idf[:], idf_in)
                for jc in range(2):
                    if jc == 1:
                        kb.wt(s, "s", tP0, tp[("wgi", 0, 7)])
                    dma(sChk, s, cC[:].rearrange("p (a c) -> p a c", a=8),
                        r3(cdft_in[:, jc * 512:(jc + 1) * 512]))
                    dma(sChk, s, cS_[:].rearrange("p (a c) -> p a c", a=8),
                        r3(sdft_in[:, jc * 512:(jc + 1) * 512]))
                    rec[("chunks", jc)] = dma(sChk, s, cN[:].rearrange("p (a c) -> p a c", a=8),
                                              r3(sneg_in[:, jc * 512:(jc + 1) * 512]))
                    for k in range(8):
                        if k >= 2 or jc == 1:
                            kb.wt(s, "s", tP0, tp[("wgi", jc, k - 2)] if k >= 2
                                  else tp[("wgi", 0, 7)])
                        dma(sWsl[k % 2], s, wsl_r[k % 2][:].rearrange("p (a c) -> p a c", a=8),
                            r3(wrT_in[:, k * 128:(k + 1) * 128]))
                        rec[("wsl", jc, k)] = dma(
                            sWsl[k % 2], s, wsl_i[k % 2][:].rearrange("p (a c) -> p a c", a=8),
                            r3(wiT_in[:, k * 128:(k + 1) * 128]))
                    for k in range(8):
                        if k >= 2 or jc == 1:
                            kb.wt(s, "s", tP0, tp[("m", jc, k - 2)] if k >= 2
                                  else tp[("m", 0, 7)])
                        dma(sCt[k % 2], s, cstC[k % 2][:].rearrange("p (a c) -> p a c", a=8),
                            r3(cdft_in[:, k * 128:(k + 1) * 128]))
                        rec[("cstat", jc, k)] = dma(
                            sCt[k % 2], s, cstS[k % 2][:].rearrange("p (a c) -> p a c", a=8),
                            r3(sdft_in[:, k * 128:(k + 1) * 128]))

            blk.sync(s0)

            def c0(c):
                for i in range(32):
                    bi, tt = divmod(i, 16)
                    if i >= 2:
                        kb.wt(c, "c", cSq, 2 * (i - 1))
                    rec[("xst", i)] = dma(sXst[i % 2], c, xst[i % 2][:],
                                          x_in[bi, tt * 128:(tt + 1) * 128, :])
                    if i >= 1:
                        j = i - 1
                        kb.wt(c, "c", *rec[("xst", j)])
                        ci = c.activation(xst[j % 2][:], xst[j % 2][:], AF.Copy,
                                          accum_out=s1c[:, j:j + 1])
                        cSq.inc(ci)
                        kb.wt(c, "c", cSq, cSq.n)
                        ci = c.activation(xst[j % 2][:], xst[j % 2][:], AF.Square,
                                          accum_out=s2c[:, j:j + 1])
                        cSq.inc(ci)
                kb.wt(c, "c", *rec[("xst", 31)])
                ci = c.activation(xst[31 % 2][:], xst[31 % 2][:], AF.Copy,
                                  accum_out=s1c[:, 31:32])
                cSq.inc(ci)
                kb.wt(c, "c", cSq, cSq.n)
                ci = c.activation(xst[31 % 2][:], xst[31 % 2][:], AF.Square,
                                  accum_out=s2c[:, 31:32])
                cSq.inc(ci)
                kb.wt(c, "c", vSt, 1)
                ci = c.activation(tst2[:], vpe[:], AF.Sqrt)
                cSq2.inc(ci)

            blk.scalar(c0)

            def t0(t):
                for jc in range(2):
                    for kp in range(8):
                        kb.wt(t, "t", *rec[("chunks", jc)])
                        kb.wt(t, "t", *rec[("wsl", jc, kp)])
                        kb.rwait(t, "t", ("pwgr", kp % 2))
                        for k in range(8):
                            t.matmul(pwgr[kp % 2][:], wsl_r[kp % 2][:, k * 128:(k + 1) * 128],
                                     cC[:, k * 512:(k + 1) * 512], start=(k == 0), stop=False)
                        for k in range(8):
                            mi = t.matmul(pwgr[kp % 2][:], wsl_i[kp % 2][:, k * 128:(k + 1) * 128],
                                          cN[:, k * 512:(k + 1) * 512], start=False, stop=(k == 7))
                        tP0.inc(mi)
                        assert tP0.n == tp[("wgr", jc, kp)]
                        kb.rset(("pwgr", kp % 2), vP0, vp[("ewgr", jc, kp)])
                        kb.rwait(t, "t", ("pwgi", kp % 2))
                        for k in range(8):
                            t.matmul(pwgi[kp % 2][:], wsl_r[kp % 2][:, k * 128:(k + 1) * 128],
                                     cS_[:, k * 512:(k + 1) * 512], start=(k == 0), stop=False)
                        for k in range(8):
                            mi = t.matmul(pwgi[kp % 2][:], wsl_i[kp % 2][:, k * 128:(k + 1) * 128],
                                          cC[:, k * 512:(k + 1) * 512], start=False, stop=(k == 7))
                        tP0.inc(mi)
                        assert tP0.n == tp[("wgi", jc, kp)]
                        kb.rset(("pwgi", kp % 2), vP0, vp[("ewgi", jc, kp)])
                    for it in range(8):
                        kb.wt(t, "t", *rec[("cstat", jc, it)])
                        kb.wt(t, "t", vP0, vp[("ewgi", jc, 7)])
                        kb.rwait(t, "t", ("pm", it % 2))
                        for kp in range(8):
                            t.matmul(pm[it % 2][:], cstC[it % 2][:, kp * 128:(kp + 1) * 128],
                                     wg_r[:, kp * 512:(kp + 1) * 512], start=(kp == 0), stop=False)
                        for kp in range(8):
                            mi = t.matmul(pm[it % 2][:], cstS[it % 2][:, kp * 128:(kp + 1) * 128],
                                          wg_i[:, kp * 512:(kp + 1) * 512], start=False, stop=(kp == 7))
                        tP0.inc(mi)
                        assert tP0.n == tp[("m", jc, it)]
                        kb.rset(("pm", it % 2), vP0, vp[("em", jc, it)])
                        if it >= 1:
                            kb.wt(t, "t", vP0, vp[("em", jc, it - 1)])
                            kb.wt(t, "t", *rec["consts"])
                            if it == 1:
                                kb.rwait(t, "t", ("pcm",))
                            mi = t.matmul(pcm[:], onese_h[:], mst[(it - 1) % 2][:],
                                          start=(it == 1), stop=False)
                            tP0.inc(mi)
                            assert tP0.n == tp[("cmm", jc, it - 1)]
                    kb.wt(t, "t", vP0, vp[("em", jc, 7)])
                    mi = t.matmul(pcm[:], onese_h[:], mst[1][:], start=False, stop=True)
                    tP0.inc(mi)
                    assert tP0.n == tp[("cmm", jc, 7)]
                    kb.rset(("pcm",), vP0, vp[("ecm", jc)])
                kb.wt(t, "t", *rec["consts"])
                for r in range(2):
                    kb.wt(t, "t", *rec[("bnc1", r)])
                    for c in range(2):
                        k4 = r * 2 + c
                        kb.rwait(t, "t", ("pcw",))
                        for a in range(4):
                            mi = t.matmul(pcw[:], onese_h[:],
                                          bnc[:, a * 1024 + c * 512:a * 1024 + (c + 1) * 512],
                                          start=(a == 0), stop=(a == 3))
                        tP0.inc(mi)
                        assert tP0.n == tp[("cwvg", k4)]
                        kb.rset(("pcw",), vP0, vp[("ecwv", k4)])
                kb.wt(t, "t", *rec["weights"])
                for c in range(4):
                    kb.rwait(t, "t", ("pcw",))
                    for et in range(8):
                        mi = t.matmul(pcw[:], onese_h[:],
                                      wqk[:, et * 2048 + c * 512:et * 2048 + (c + 1) * 512],
                                      start=(et == 0), stop=(et == 7))
                    tP0.inc(mi)
                    assert tP0.n == tp[("cwqk", c)]
                    kb.rset(("pcw",), vP0, vp[("ecwqk", c)])

            blk.tensor(t0)

            def v0(v):
                for jc in range(2):
                    for kp in range(8):
                        kb.wt(v, "v", tP0, tp[("wgr", jc, kp)])
                        vi = v.tensor_scalar(wg_r[:, kp * 512:(kp + 1) * 512], pwgr[kp % 2][:],
                                             1.0 / 1024.0, None, ALU.mult)
                        vP0.inc(vi)
                        assert vP0.n == vp[("ewgr", jc, kp)]
                        kb.wt(v, "v", tP0, tp[("wgi", jc, kp)])
                        vi = v.tensor_scalar(wg_i[:, kp * 512:(kp + 1) * 512], pwgi[kp % 2][:],
                                             1.0 / 1024.0, None, ALU.mult)
                        vP0.inc(vi)
                        assert vP0.n == vp[("ewgi", jc, kp)]
                    if jc == 1:
                        kb.wt(v, "v", cSq, 64)
                        def step(vi):
                            vP0.inc(vi)
                            kb.wt(v, "v", vP0, vP0.n)
                        step(v.tensor_scalar(negm[:], s1c[:], -1.0 / 1024.0, None, ALU.mult))
                        step(v.tensor_tensor(tst1[:], negm[:], negm[:], ALU.mult))
                        step(v.tensor_scalar(tst2[:], s2c[:], 1.0 / 1024.0, None, ALU.mult))
                        step(v.tensor_tensor(tst3[:], tst2[:], tst1[:], ALU.subtract))
                        vi = v.tensor_scalar(vpe[:], tst3[:], EPS, None, ALU.add)
                        vSt.inc(vi)
                        kb.wt(v, "v", vSt, 1)
                        kb.wt(v, "v", cSq2, 1)
                        step(v.reciprocal(tst1[:], tst2[:]))
                        step(v.tensor_tensor(tst3[:], tst1[:], tst1[:], ALU.mult))
                        step(v.tensor_tensor(tst3[:], tst3[:], vpe[:], ALU.mult))
                        step(v.tensor_scalar(tst3[:], tst3[:], -0.5, 1.5, ALU.mult, ALU.add))
                        vi = v.tensor_tensor(a_sb[:], tst1[:], tst3[:], ALU.mult)
                        vSt.inc(vi)
                    for it in range(8):
                        kb.wt(v, "v", tP0, tp[("m", jc, it)])
                        if it >= 2:
                            kb.wt(v, "v", *rec[("mdma", jc, it - 2)])
                            kb.wt(v, "v", tP0, tp[("cmm", jc, it - 2)])
                        elif jc == 1:
                            kb.wt(v, "v", *rec[("mdma", 0, 6 + it)])
                            kb.wt(v, "v", tP0, tp[("cmm", 0, 6 + it)])
                        vi = v.tensor_copy(mst[it % 2][:], pm[it % 2][:])
                        vP0.inc(vi)
                        assert vP0.n == vp[("em", jc, it)]
                    kb.wt(v, "v", tP0, tp[("cmm", jc, 7)])
                    vi = v.tensor_copy(cm4[:, jc * 512:(jc + 1) * 512], pcm[:])
                    vP0.inc(vi)
                    assert vP0.n == vp[("ecm", jc)]
                for k4 in range(4):
                    r, c = divmod(k4, 2)
                    kb.wt(v, "v", tP0, tp[("cwvg", k4)])
                    if r == 0:
                        vi = v.tensor_copy(cwv4[:, c * 512:(c + 1) * 512], pcw[:])
                    else:
                        kb.wt(v, "v", vP0, vp[("ecwv", c)])
                        vi = v.tensor_tensor(cwv4[:, c * 512:(c + 1) * 512], pcw[:],
                                             cwv4[:, c * 512:(c + 1) * 512], ALU.add)
                    vP0.inc(vi)
                    assert vP0.n == vp[("ecwv", k4)]
                for c in range(4):
                    kb.wt(v, "v", tP0, tp[("cwqk", c)])
                    vi = v.tensor_copy(cwqk4[:, c * 512:(c + 1) * 512], pcw[:])
                    vP0.inc(vi)
                    assert vP0.n == vp[("ecwqk", c)]
            blk.vector(v0)



        rec["p0done"] = [(tP0, tp["end"]), (vP0, vP0.n), (cSq, 64), (cSq2, 1),
                         (vSt, 2), (sBnc[0], sBnc[0].n), (sBnc[1], sBnc[1].n),
                         (oM[0], oM[0].n), (oM[1], oM[1].n)]

        def p0wait(eng, ename):
            for se, va in rec["p0done"]:
                kb.wt(eng, ename, se, va)

        K_sb = sb("K_sb", [128, 8 * 2048], BF16)
        V_sb = sb("V_sb", [128, 16 * 1024], BF16)
        Qb = [sb(f"Qb{i}", [128, 8 * 512], BF16) for i in range(2)]
        attnT = sb("attnT", [128, 8 * 512], F16)
        xTh = [sb(f"xTh{i}", [128, 8 * 512], F16) for i in range(2)]
        xres = [sb(f"xres{i}", [128, 512], F32) for i in range(4)]
        P_sb = sb("P_sb", [128, 2 * 1024], BF16)
        Ab = [sb(f"Ab{i}", [128, 512], F32) for i in range(2)]
        rt1 = sb("rt1", [128, 512], F32)
        rt2 = sb("rt2", [128, 512], BF16)
        ostage = [sb(f"ostage{i}", [128, 512], F32) for i in range(4)]
        NM = 6
        mstr = [sb(f"mstr{i}", [128, 512], F16) for i in range(NM)]
        vstr = [sb(f"vstr{i}", [128, 512], F16) for i in range(6)]
        wstr = [sb(f"wstr{i}", [128, 512], F16) for i in range(6)]
        pbig = es.enter_context(nc.psum_tensor("pbig", [128, 2048], F32))
        pb = [es.enter_context(nc.psum_tensor(f"pb{i}", [128, 512], F32)) for i in range(4)]

        def pbank(i, c0=0, c1=512):
            if i < 4:
                return pbig[:, i * 512 + c0:i * 512 + c1]
            return pb[i - 4][:, c0:c1]

        def preT(t):
            p0wait(t, "t")
            kb.wt(t, "t", vSt, 2)
            kb.wt(t, "t", *rec["idf"])
            t.transpose(pbig[0:32, 0:128], a_sb[:, 0:32], idf[:])
            mi = t.transpose(pbig[0:32, 512:640], negm[:, 0:32], idf[:])
            tB.inc(mi)
            rec["statT"] = tB.n

        blk.tensor(preT)

        def preV(v):
            p0wait(v, "v")
            kb.wt(v, "v", tB, rec["statT"])
            v.tensor_copy(stga_all[:], pbig[0:32, 0:128])
            vi = v.tensor_copy(stgn_all[:], pbig[0:32, 512:640])
            vB.inc(vi)
            rec["statC"] = vB.n

        blk.vector(preV)
        TS, VS, CS, SS, GS = [], [], [], [], []
        TC, VC, CC = {}, {}, {}
        TBASE, VBASE = 1, 1
        res2 = {("b", 0): ("S", vB, 1), ("b", 1): ("S", vB, 1)}
        pbk = {}
        dmac = {}
        pcnt = {}
        cn = [0]
        NMB = NM

        dmanode = {}

        def pdma(sem):
            pcnt[id(sem)] = pcnt.get(id(sem), 0) + 16
            return ("S", sem, pcnt[id(sem)])

        def r2w(key):
            e = res2.get(key)
            return [e] if e is not None else []

        def tun(key, w):
            TS.append((key, w)); TC[key] = TBASE + len(TS)

        def vun(key, w):
            VS.append((key, w)); VC[key] = VBASE + len(VS)

        def cun(key, w, inc=1):
            CS.append((key, w))
            if inc:
                cn[0] += 1
                CC[key] = cn[0]

        def blkinfo(g):
            b, qb = divmod(g, 4)
            return b, qb, qb * 512, 4 * (qb + 1)

        pool = [6]

        def tpool(key, w):
            pool[0] = 13 - pool[0]
            bank = pool[0]
            pbk[key] = bank
            tun(key, w + r2w(("b", bank)))
            return bank

        CONSTS = ("S",) + rec["consts"]
        STATC = ("S", vB, 1)

        def put_gdma(key, sem, ndma, w):
            GS.append((key, w))
            for _ in range(ndma):
                dmac[key] = pdma(sem)

        def put_a4(g):
            par = g % 2
            w = [STATC, CONSTS] + r2w(("a4r", par)) + r2w(("nm4b", par)) + r2w(("nm4r", par))
            put_gdma(("a4", g), sA4[g % 2], 12, w)
            res2[("a4r", par)] = ("t", ("ab", g))
            res2[("nm4b", par)] = ("t", ("vf", g, 1, 3))
            res2[("nm4r", par)] = ("t", ("rf", g, 1, 3))

        def put_xtb(g):
            put_gdma(("xtb", g), sXtb[g % 2], 1, r2w(("xTb", g % 2)))
            res2[("xTb", g % 2)] = ("t", ("rm", g, 1, 7))

        def put_ropeld(g):
            put_gdma(("ropeld", g), sXtr[g % 2], 2, r2w(("rope", g % 2)))
            res2[("rope", g % 2)] = ("v", ("cos", g, 15))

        def put_xres(g):
            for ch in range(2):
                for tt in range(4):
                    put_gdma(("xres", g, ch, tt), sXr[tt], 1, r2w(("xresh", tt)))
                    res2[("xresh", tt)] = ("v", ("o1", g, ch, tt))

        def put_mdmas(g):
            for u in range(16):
                ch, et = divmod(u, 8)
                km = (g * 16 + u) % NMB
                w = r2w(("mstr", km))
                if g == 0 and u == 0:
                    w = w + [("S",) + rec[("mdma", 1, 7)], ("S",) + rec[("mdma", 1, 6)]]
                SS.append((("mdma", g, u), w))
                dmac[("mdma", g, u)] = pdma(sMs[km])
                res2[("mstr", km)] = ("t", ("rm", g, ch, et))

        def put_vdmas(g):
            for k in range(16):
                ch, et = divmod(k, 8)
                if k == 14:
                    for tt in range(4):
                        cun(("vev", g, 0, tt), [("t", ("vf", g, 0, tt))])
                w = r2w(("vstr", k % 6))
                if g == 0 and k == 0:
                    w = w + [("S",) + rec["wbf_a"], ("S",) + rec["wbf_b"]]
                cun(("vdma", g, k), w, inc=0)
                dmac[("vdma", g, k)] = pdma(sVs[k % 6])
                res2[("vstr", k % 6)] = ("t", ("vm", g, ch, et))

        def put_wdmas(g):
            for k in range(16):
                ch, hp = divmod(k, 8)
                w = r2w(("wstr", k % 6))
                cun(("wdma", g, k), w, inc=0)
                dmac[("wdma", g, k)] = pdma(sWs[k % 6])
                res2[("wstr", k % 6)] = ("t", ("rw", g, ch, hp))

        def put_ab(g):
            par = g % 2
            bk = tpool(("ab", g), [dmac[("a4", g)]])
            res2[("b", bk)] = ("v", ("abcopy", g))
            vun(("abcopy", g), [("t", ("ab", g))] + r2w(("Ab", par)))
            res2[("Ab", par)] = ("v", ("evac", g, 15))

        def put_qkft(g, ft):
            bk = tpool(("qk", g, ft), [dmac[("xtb", g)], dmac[("a4", g)]])
            res2[("b", bk)] = ("v", ("evac", g, ft))
            vun(("evac", g, ft), [("t", ("qk", g, ft)), ("v", ("abcopy", g))])
            bk2 = tpool(("T", g, ft), [("v", ("evac", g, ft))])
            res2[("b", bk2)] = ("v", ("t1", g, ft))
            vun(("t1", g, ft), [("t", ("T", g, ft)), dmac[("ropeld", g)]]
                + r2w(("rt2",)))
            res2[("rt2",)] = ("v", ("rope", g, ft))
            vun(("cos", g, ft), [("v", ("evac", g, ft))])
            vun(("rope", g, ft), [("v", ("cos", g, ft))])

        def put_vphase(g):
            for ch in range(2):
                bb = 0 if (g == 0 and ch == 1) else 4
                for et in range(8):
                    k = ch * 8 + et
                    w = [dmac[("vdma", g, k)], dmac[("xtb", g)]]
                    if et == 0:
                        for tt in range(4):
                            w += r2w(("b", bb + tt))
                    tun(("vm", g, ch, et), w)
                for tt in range(4):
                    tun(("vf", g, ch, tt), [dmac[("a4", g)]])
                    res2[("b", bb + tt)] = ("c", ("vev", g, ch, tt))

        def put_rez(g):
            for ch in range(2):
                for et in range(8):
                    w = [dmac[("xtb", g)], dmac[("mdma", g, ch * 8 + et)]]
                    if et == 0:
                        for tt in range(4):
                            w += r2w(("b", tt))
                    tun(("rm", g, ch, et), w)
                for tt in range(4):
                    tun(("rf", g, ch, tt), [dmac[("a4", g)]])
                    res2[("b", tt)] = ("v", ("o1", g, ch, tt))
                    vun(("o1", g, ch, tt),
                        [("t", ("rf", g, ch, tt)), dmac[("xres", g, ch, tt)]]
                        + r2w(("ostage", tt)))
                for hp in range(8):
                    w = [dmac[("wdma", g, ch * 8 + hp)], ("v", ("norm", g, hp))]
                    if hp == 0:
                        for tt in range(4):
                            w += r2w(("b", 4 + tt))
                    tun(("rw", g, ch, hp), w)
                for tt in range(4):
                    tun(("rb", g, ch, tt), [])
                    res2[("b", 4 + tt)] = ("v", ("o2", g, ch, tt))
                    vun(("o2", g, ch, tt),
                        [("t", ("rb", g, ch, tt)), ("v", ("o1", g, ch, tt))])
                    SS.append((("out", g, ch, tt), [("v", ("o2", g, ch, tt))]))
                    dmac[("out", g, ch, tt)] = pdma(oST[tt])
                    res2[("ostage", tt)] = dmac[("out", g, ch, tt)]

        def put_att(g, iq):
            b_, qb, S0, nkb = blkinfo(g)
            U = 8 * nkb
            ip = [0]

            def pops(n):
                while n > 0 and ip[0] < len(iq):
                    k = iq[ip[0]]; ip[0] += 1; n -= 1
                    if k[0] == "ab":
                        put_ab(k[1])
                    else:
                        put_qkft(k[1], k[2])

            def put_sc(u):
                hp, kbi = divmod(u, nkb)
                par = u % 2
                w = []
                if kbi == 0:
                    w.append(("v", ("rope", g, 8 + hp)))
                w += r2w(("b", 2 * par)) + r2w(("b", 2 * par + 1))
                tun(("sc", g, u), w)
                res2[("b", 2 * par)] = ("c", ("exp", g, u))
                res2[("b", 2 * par + 1)] = ("c", ("exp", g, u))
                ew = [("t", ("sc", g, u))]
                if u >= 2:
                    ew.append(("t", ("ad", g, u - 2)))
                elif g > 0:
                    Up = 8 * (4 * (((g - 1) % 4) + 1))
                    ew.append(("t", ("ad", g - 1, Up - 2 + u)))
                cun(("exp", g, u), ew)
                j = kbi - 4 * qb
                if j >= 0:
                    vun(("mask", g, hp, j), [("c", ("exp", g, u))])

            def put_ad(u):
                hp, kbi = divmod(u, nkb)
                w = [("c", ("exp", g, u))]
                j = kbi - 4 * qb
                if j >= 0:
                    w.append(("v", ("mask", g, hp, j)))
                if kbi == 0:
                    w += r2w(("b", 4)) + r2w(("b", 5))
                tun(("ad", g, u), w)
                if kbi == 0:
                    res2[("b", 4)] = ("v", ("norm", g, hp))
                    res2[("b", 5)] = ("v", ("recip", g, hp))
                if kbi == nkb - 1:
                    vun(("recip", g, hp), [("t", ("ad", g, u))] + r2w(("rt1",)))
                    res2[("rt1",)] = ("v", ("norm", g, hp))
                    vun(("norm", g, hp), [("v", ("recip", g, hp))])

            for u in range(U):
                put_sc(u)
                if u >= 1:
                    ua = u - 1
                    hpa, kbia = divmod(ua, nkb)
                    pops(4 if (kbia == 0 and hpa > 0) else 1)
                    put_ad(ua)
            put_ad(U - 1)
            pops(len(iq))

        put_a4(0); put_ropeld(0); put_xtb(0); put_xtb(1)
        put_vdmas(0)
        put_ab(0)
        for ft in range(16):
            put_qkft(0, ft)
        put_vphase(0)
        for g in range(NBLK):
            if g + 1 < NBLK:
                put_a4(g + 1)
                put_ropeld(g + 1)
            put_xres(g)
            if g + 2 < NBLK:
                put_xtb(g + 2)
            put_mdmas(g)
            for tt in range(4):
                cun(("vev", g, 1, tt), [("t", ("vf", g, 1, tt))])
            if g == NBLK - 1:
                iq, dq = [], []
            elif g == 3:
                iq = [("ab", 4)] + [("qkT", 4, ft) for ft in range(8)]
                dq = [("qkT", 4, ft) for ft in range(8, 16)]
            else:
                iq = [("ab", g + 1)] + [("qkT", g + 1, ft) for ft in range(16)]
                dq = []
            put_att(g, iq)
            put_wdmas(g)
            if g + 1 < NBLK:
                put_vdmas(g + 1)
            for k in dq:
                put_qkft(k[1], k[2])
            put_rez(g)
            if g + 1 < NBLK:
                put_vphase(g + 1)

        streams = {"T": TS, "V": VS, "C": CS, "S": SS, "G": GS}
        for ei, (enm, st) in enumerate([("C", CS), ("S", SS), ("G", GS)]):
            for i, (key, w) in enumerate(st):
                if key[0] in ("vdma", "wdma", "mdma", "out", "a4", "xtb",
                              "ropeld", "xres"):
                    dmanode[key] = (enm, i)
        kix = {}
        for enm, st in streams.items():
            for i, (key, w) in enumerate(st):
                if key[0] not in ("vdma", "wdma", "mdma", "out", "a4", "xtb",
                                  "ropeld", "xres"):
                    kix[key] = (enm, i)
        dmabykey = {}
        for key, e in dmac.items():
            dmabykey[(id(e[1]), e[2])] = dmanode[key]
        nnodes = {enm: len(st) for enm, st in streams.items()}
        adj = {}

        def addedge(a_, b_):
            adj.setdefault(a_, []).append(b_)

        for enm, st in streams.items():
            for i in range(1, len(st)):
                addedge((enm, i - 1), (enm, i))
            for i, (key, w) in enumerate(st):
                for e in w:
                    if e[0] in ("t", "v", "c"):
                        src_ = kix[e[1]]
                    else:
                        src_ = dmabykey.get((id(e[1]), e[2]))
                        if src_ is None:
                            continue
                    addedge(src_, (enm, i))
        indeg = {}
        for a_, bs in adj.items():
            for b_ in bs:
                indeg[b_] = indeg.get(b_, 0) + 1
        allnodes = [(enm, i) for enm, st in streams.items() for i in range(len(st))]
        from collections import deque
        dq_ = deque(n for n in allnodes if indeg.get(n, 0) == 0)
        seen = 0
        indeg2 = dict(indeg)
        while dq_:
            n = dq_.popleft(); seen += 1
            for b_ in adj.get(n, []):
                indeg2[b_] -= 1
                if indeg2[b_] == 0:
                    dq_.append(b_)
        if seen != len(allnodes):
            stuck = set(n for n in allnodes if indeg2.get(n, 0) > 0)
            radj = {}
            for a_, bs in adj.items():
                for b_ in bs:
                    if a_ in stuck and b_ in stuck:
                        radj.setdefault(b_, []).append(a_)
            n = next(iter(stuck)); path = [n]; pos = {n: 0}
            while True:
                n = radj[n][0]
                if n in pos:
                    cyc = path[pos[n]:]
                    break
                pos[n] = len(path); path.append(n)
            det = [(x, streams[x[0]][x[1]][0]) for x in cyc]
            raise AssertionError(f"schedule cycle ({len(cyc)}): {det}")

        def wemit(eng, en, w):
            for e in w:
                if e[0] == "t":
                    kb.wt(eng, en, tB, TC[e[1]])
                elif e[0] == "v":
                    kb.wt(eng, en, vB, VC[e[1]])
                elif e[0] == "c":
                    kb.wt(eng, en, cB, CC[e[1]])
                else:
                    kb.wt(eng, en, e[1], e[2])

        def qkdest(g, ft):
            S0 = (g % 4) * 512
            return (Qb[g % 2][:, ft * 512:(ft + 1) * 512] if ft < 8
                    else K_sb[:, (ft - 8) * 2048 + S0:(ft - 8) * 2048 + S0 + 512])

        def dmach(r_, key):
            assert r_[0] is dmac[key][1] and r_[1] == dmac[key][2], key

        def vbank(g, ch, tt):
            if g == 0 and ch == 1:
                return pbig[:, tt * 512:(tt + 1) * 512]
            return pb[tt][:]

        def emit_t(t):
            p0wait(t, "t")
            kb.wt(t, "t", *rec["weights"])
            kb.wt(t, "t", *rec["consts"])
            kb.wt(t, "t", tP0, tp[("cwqk", 3)])
            kb.wt(t, "t", vP0, vp["end"])
            for key, w in TS:
                wemit(t, "t", w)
                kind = key[0]
                if kind == "ab":
                    g = key[1]
                    mi = t.matmul(pbank(pbk[key]), ones4r[:], a4r[g % 2][:],
                                  start=True, stop=True)
                elif kind == "qk":
                    g, ft = key[1], key[2]
                    for et in range(8):
                        t.matmul(pbank(pbk[key]),
                                 wqk[:, et * 2048 + ft * 128:et * 2048 + (ft + 1) * 128],
                                 xTh[g % 2][:, et * 512:(et + 1) * 512],
                                 start=(et == 0), stop=False)
                    mi = t.matmul(pbank(pbk[key]), cwqk4[:, ft * 128:(ft + 1) * 128],
                                  nm4b[g % 2][:], start=False, stop=True)
                elif kind == "T":
                    g, ft = key[1], key[2]
                    mi = t.matmul(pbank(pbk[key]), tmat[:], qkdest(g, ft),
                                  start=True, stop=True)
                elif kind == "vm":
                    g, ch, et = key[1], key[2], key[3]
                    for tt in range(4):
                        mi = t.matmul(vbank(g, ch, tt),
                                      xTh[g % 2][:, et * 512 + tt * 128:et * 512 + tt * 128 + 128],
                                      vstr[(ch * 8 + et) % 6][:], start=(et == 0), stop=False)
                elif kind == "vf":
                    g, ch, tt = key[1], key[2], key[3]
                    mi = t.matmul(vbank(g, ch, tt), nm4b[g % 2][:, tt * 128:(tt + 1) * 128],
                                  cwv4[:, ch * 512:(ch + 1) * 512], start=False, stop=True)
                elif kind == "sc":
                    g, u = key[1], key[2]
                    b_, qb, S0, nkb = blkinfo(g)
                    hp, kbi = divmod(u, nkb)
                    par = u % 2
                    cs = max(0, kbi - 4 * qb) * 128
                    t.matmul(pbig[:, par * 1024 + cs:par * 1024 + 512],
                             K_sb[0:64, hp * 2048 + kbi * 128:hp * 2048 + (kbi + 1) * 128],
                             Qb[g % 2][0:64, hp * 512 + cs:(hp + 1) * 512],
                             start=True, stop=True, tile_position=(0, 0))
                    mi = t.matmul(pbig[:, par * 1024 + 512 + cs:par * 1024 + 1024],
                                  K_sb[64:128, hp * 2048 + kbi * 128:hp * 2048 + (kbi + 1) * 128],
                                  Qb[g % 2][64:128, hp * 512 + cs:(hp + 1) * 512],
                                  start=True, stop=True, tile_position=(64, 0))
                elif kind == "ad":
                    g, u = key[1], key[2]
                    b_, qb, S0, nkb = blkinfo(g)
                    hp, kbi = divmod(u, nkb)
                    par = u % 2
                    cs = max(0, kbi - 4 * qb) * 128
                    t.matmul(pb[0][0:64, cs:512],
                             V_sb[:, kbi * 1024 + (2 * hp) * 64:kbi * 1024 + (2 * hp) * 64 + 64],
                             P_sb[:, par * 1024 + cs:par * 1024 + 512],
                             start=(kbi == 0), stop=(kbi == nkb - 1), tile_position=(0, 0),
                             skip_group_check=True)
                    t.matmul(pb[0][64:128, cs:512],
                             V_sb[:, kbi * 1024 + (2 * hp + 1) * 64:kbi * 1024 + (2 * hp + 1) * 64 + 64],
                             P_sb[:, par * 1024 + 512 + cs:par * 1024 + 1024],
                             start=(kbi == 0), stop=(kbi == nkb - 1), tile_position=(0, 64),
                             skip_group_check=True)
                    t.matmul(pb[1][0:64, cs:512], ones64[:],
                             P_sb[:, par * 1024 + cs:par * 1024 + 512],
                             start=(kbi == 0), stop=(kbi == nkb - 1), tile_position=(0, 0),
                             skip_group_check=True)
                    mi = t.matmul(pb[1][64:128, cs:512], ones64[:],
                                  P_sb[:, par * 1024 + 512 + cs:par * 1024 + 1024],
                                  start=(kbi == 0), stop=(kbi == nkb - 1), tile_position=(0, 64),
                                  skip_group_check=True)
                elif kind == "rm":
                    g, ch, et = key[1], key[2], key[3]
                    km = (g * 16 + ch * 8 + et) % NMB
                    for tt in range(4):
                        mi = t.matmul(pbig[:, tt * 512:(tt + 1) * 512],
                                      xTh[g % 2][:, et * 512 + tt * 128:et * 512 + tt * 128 + 128],
                                      mstr[km][:], start=(et == 0), stop=False)
                elif kind == "rf":
                    g, ch, tt = key[1], key[2], key[3]
                    mi = t.matmul(pbig[:, tt * 512:(tt + 1) * 512],
                                  nm4r[g % 2][:, tt * 128:(tt + 1) * 128],
                                  cm4[:, ch * 512:(ch + 1) * 512], start=False, stop=True)
                elif kind == "rw":
                    g, ch, hp = key[1], key[2], key[3]
                    for tt in range(4):
                        mi = t.matmul(pb[tt][:],
                                      attnT[:, hp * 512 + tt * 128:hp * 512 + (tt + 1) * 128],
                                      wstr[(ch * 8 + hp) % 6][:], start=(hp == 0), stop=False)
                else:
                    g, ch, tt = key[1], key[2], key[3]
                    mi = t.matmul(pb[tt][:], ones1r[:], biasr[0:1, ch * 512:(ch + 1) * 512],
                                  start=False, stop=True)
                tB.inc(mi)
                assert tB.n == TC[key], key

        blk.tensor(emit_t)

        def emit_v(v):
            p0wait(v, "v")
            for key, w in VS:
                wemit(v, "v", w)
                kind = key[0]
                if kind == "abcopy":
                    g = key[1]
                    vi = v.tensor_copy(Ab[g % 2][:], pbank(pbk[("ab", g)]))
                elif kind == "evac":
                    g, ft = key[1], key[2]
                    vi = v.tensor_tensor(qkdest(g, ft), pbank(pbk[("qk", g, ft)]),
                                         Ab[g % 2][:], ALU.mult)
                elif kind == "t1":
                    g, ft = key[1], key[2]
                    vi = v.tensor_tensor(rt2[:], pbank(pbk[("T", g, ft)]),
                                         rsn[g % 2][:], ALU.mult)
                elif kind == "cos":
                    g, ft = key[1], key[2]
                    d_ = qkdest(g, ft)
                    vi = v.tensor_tensor(d_, d_, rcs[g % 2][:], ALU.mult)
                elif kind == "rope":
                    g, ft = key[1], key[2]
                    d_ = qkdest(g, ft)
                    vi = v.tensor_tensor(d_, d_, rt2[:], ALU.add)
                elif kind == "mask":
                    g, hp, j = key[1], key[2], key[3]
                    b_, qb, S0, nkb = blkinfo(g)
                    u = hp * nkb + 4 * qb + j
                    par = u % 2
                    v.tensor_tensor(P_sb[:, par * 1024 + j * 128:par * 1024 + (j + 1) * 128],
                                    P_sb[:, par * 1024 + j * 128:par * 1024 + (j + 1) * 128],
                                    maskt[:], ALU.mult)
                    vi = v.tensor_tensor(
                        P_sb[:, par * 1024 + 512 + j * 128:par * 1024 + 512 + (j + 1) * 128],
                        P_sb[:, par * 1024 + 512 + j * 128:par * 1024 + 512 + (j + 1) * 128],
                        maskt[:], ALU.mult)
                elif kind == "recip":
                    vi = v.reciprocal(rt1[:], pb[1][:])
                elif kind == "norm":
                    g, hp = key[1], key[2]
                    vi = v.tensor_tensor(attnT[:, hp * 512:(hp + 1) * 512], pb[0][:],
                                         rt1[:], ALU.mult)
                elif kind == "o1":
                    g, ch, tt = key[1], key[2], key[3]
                    gi0 = (g // 4) * 16 + (g % 4) * 4
                    vi = v.scalar_tensor_tensor(ostage[tt][:],
                                                pbig[:, tt * 512:(tt + 1) * 512],
                                                a_sb[:, gi0 + tt:gi0 + tt + 1],
                                                xres[tt][:], ALU.mult, ALU.add)
                else:
                    g, ch, tt = key[1], key[2], key[3]
                    vi = v.scalar_tensor_tensor(ostage[tt][:], pb[tt][:], 1.0,
                                                ostage[tt][:], ALU.mult, ALU.add)
                vB.inc(vi)
                assert vB.n == VC[key], key

        blk.vector(emit_v)

        def emit_c(c):
            p0wait(c, "c")
            for key, w in CS:
                wemit(c, "c", w)
                kind = key[0]
                if kind == "vdma":
                    g, k = key[1], key[2]
                    ch, et = divmod(k, 8)
                    r_ = dma(sVs[k % 6], c, vstr[k % 6][:],
                             wvbf[et * 128:(et + 1) * 128, ch * 512:(ch + 1) * 512])
                    dmach(r_, key)
                    continue
                if kind == "wdma":
                    g, k = key[1], key[2]
                    ch, hp = divmod(k, 8)
                    r_ = dma(sWs[k % 6], c, wstr[k % 6][:],
                             woutbf[hp * 128:(hp + 1) * 128, ch * 512:(ch + 1) * 512])
                    dmach(r_, key)
                    continue
                if kind == "vev":
                    g, ch, tt = key[1], key[2], key[3]
                    b_, qb, S0, nkb = blkinfo(g)
                    gi0 = b_ * 16 + qb * 4
                    ci = c.activation(
                        V_sb[:, (qb * 4 + tt) * 1024 + ch * 512:(qb * 4 + tt) * 1024 + (ch + 1) * 512],
                        vbank(g, ch, tt), AF.Copy, scale=a_sb[:, gi0 + tt:gi0 + tt + 1])
                else:
                    g, u = key[1], key[2]
                    b_, qb, S0, nkb = blkinfo(g)
                    hp, kbi = divmod(u, nkb)
                    j = kbi - 4 * qb
                    cs = max(0, j) * 128
                    par = u % 2
                    if cs == 0:
                        ci = c.activation(P_sb[:, par * 1024:par * 1024 + 1024],
                                          pbig[:, par * 1024:par * 1024 + 1024],
                                          AF.Exp, scale=0.125)
                    else:
                        c.activation(P_sb[:, par * 1024 + cs:par * 1024 + 512],
                                     pbig[:, par * 1024 + cs:par * 1024 + 512],
                                     AF.Exp, scale=0.125)
                        ci = c.activation(P_sb[:, par * 1024 + 512 + cs:par * 1024 + 1024],
                                          pbig[:, par * 1024 + 512 + cs:par * 1024 + 1024],
                                          AF.Exp, scale=0.125)
                cB.inc(ci)
                assert cB.n == CC[key], key

        blk.scalar(emit_c)

        def emit_s(s):
            p0wait(s, "s")
            for key, w in SS:
                wemit(s, "s", w)
                kind = key[0]
                if kind == "mdma":
                    g, u = key[1], key[2]
                    ch, et = divmod(u, 8)
                    km = (g * 16 + u) % NMB
                    r_ = dma(sMs[km], s, mstr[km][:],
                             mhbm[et * 128:(et + 1) * 128, ch * 512:(ch + 1) * 512])
                else:
                    g, ch, tt = key[1], key[2], key[3]
                    b_, qb, S0, nkb = blkinfo(g)
                    r_ = dma(oST[tt], s,
                             y_out[b_, S0 + tt * 128:S0 + (tt + 1) * 128,
                                   ch * 512:(ch + 1) * 512],
                             ostage[tt][:])
                dmach(r_, key)

        blk.sync(emit_s)

        def emit_g(g_):
            p0wait(g_, "g")
            for key, w in GS:
                wemit(g_, "g", w)
                kind = key[0]
                if kind == "a4":
                    g = key[1]
                    par = g % 2
                    gi0 = (g // 4) * 16 + (g % 4) * 4
                    for tt in range(4):
                        gt4 = gi0 + tt
                        dma(sA4[par], g_, a4r[par][tt:tt + 1, tt * 128:(tt + 1) * 128],
                            stga_all[gt4:gt4 + 1, :])
                        dma(sA4[par], g_, nm4b[par][tt:tt + 1, tt * 128:(tt + 1) * 128],
                            stgn_all[gt4:gt4 + 1, :])
                        r_ = dma(sA4[par], g_, nm4r[par][tt:tt + 1, tt * 128:(tt + 1) * 128],
                                 stgn_all[gt4:gt4 + 1, :])
                elif kind == "xtb":
                    g = key[1]
                    b_, qb, S0, nkb = blkinfo(g)
                    r_ = dma(sXtb[g % 2], g_, xTh[g % 2][:].rearrange("p (a c) -> p a c", a=8),
                             r3(xT_in[b_, :, S0:S0 + 512]))
                elif kind == "ropeld":
                    g = key[1]
                    b_, qb, S0, nkb = blkinfo(g)
                    dma(sXtr[g % 2], g_, rcs[g % 2][:], rcos_in[:, S0:S0 + 512])
                    r_ = dma(sXtr[g % 2], g_, rsn[g % 2][:], rsin_in[:, S0:S0 + 512])
                else:
                    g, ch, tt = key[1], key[2], key[3]
                    b_, qb, S0, nkb = blkinfo(g)
                    r_ = dma(sXr[tt], g_, xres[tt][:],
                             x_in[b_, S0 + tt * 128:S0 + (tt + 1) * 128,
                                  ch * 512:(ch + 1) * 512])
                dmach(r_, key)

        blk.gpsimd(emit_g)

        if DEBUG:
            dbgs = {}
            for nm, t_ in [("dbg_a", a_sb), ("dbg_negm", negm), ("dbg_s1", s1c),
                           ("dbg_s2", s2c), ("dbg_vpe", vpe),
                           ("dbg_cwqk", cwqk4), ("dbg_cwv", cwv4), ("dbg_cm", cm4),
                           ("dbg_K", K_sb), ("dbg_V", V_sb), ("dbg_Q", Qb[0]),
                           ("dbg_attnT", attnT), ("dbg_xTb", xTh[0]), ("dbg_Ab", Ab[0]),
                           ("dbg_a4r", a4r[0]), ("dbg_nm4b", nm4b[0]), ("dbg_nm4r", nm4r[0]),
                           ("dbg_P", P_sb), ("dbg_wqk", wqk)]:
                sh = list(t_.shape)
                dt_ = {mybir.dt.float32: F32, mybir.dt.float32r: F32,
                       mybir.dt.bfloat16: F32}[t_.dtype]
                dbgs[nm] = (ap(nc.dram_tensor(nm, sh, F32, kind="ExternalOutput")), t_)

        def gend(g_):
            for s_ in oST:
                g_.wait_ge(s_.h, s_.n)
            if DEBUG:
                g_.wait_ge(tB.h, tB.n)
                g_.wait_ge(vB.h, vB.n)
                g_.wait_ge(cB.h, cB.n)
                for nm, (dr, t_) in dbgs.items():
                    g_.dma_start(out=dr, in_=t_[:]).then_inc(sCst.h, 16)
                    sCst.n += 16
                g_.wait_ge(sCst.h, sCst.n)

        blk.gpsimd(gend)

    return nc


def _perm_rows():
    qp, kpr, vp = [], [], []
    for h in range(NH):
        for d in range(HD):
            base = 768 * (h // 4) + 64 * (h % 4) + d
            qp.append(base)
            vp.append(base + 256)
            kpr.append(base + 512)
    return qp, kpr, vp


def _constants():
    j = np.arange(D, dtype=np.int64)
    m = np.mod(np.outer(j, j), D).astype(np.float64)
    ang = 2.0 * np.pi * m / D
    cdft = np.cos(ang).astype(np.float32)
    sdft = np.sin(ang).astype(np.float32)
    sneg = (-np.sin(ang)).astype(np.float32)
    inv = 1.0 / 10000.0 ** (np.arange(0, RD, 2, dtype=np.float64) / RD)
    pos = np.arange(S, dtype=np.float64)
    sinu = np.outer(inv, pos)
    rcos = np.ones((128, S), dtype=np.float32)
    rsin = np.zeros((128, S), dtype=np.float32)
    for p in range(128):
        d = p % 64
        if d < RD:
            i = d // 2
            rcos[p] = np.cos(sinu[i]).astype(np.float32)
            rsin[p] = (np.sin(sinu[i]) * (-1.0 if d % 2 == 0 else 1.0)).astype(np.float32)
    tmat = np.zeros((128, 128), dtype=np.float32)
    for p in range(128):
        if (p % 64) < RD:
            tmat[p, p ^ 1] = 1.0
    mask = np.tril(np.ones((128, 128), dtype=np.float32)).T
    idf = np.eye(128, dtype=np.float32)
    return cdft, sdft, sneg, rcos, rsin, tmat, mask, idf


_NC_CACHE = {}


def kernel(L0_attn_hidden_states, L0_attn_w_qkv, L0_attn_w_out,
           L0_ffn_w_freq_real, L0_ffn_w_freq_imag, L0_ffn_bias):
    x = np.ascontiguousarray(np.asarray(L0_attn_hidden_states, dtype=np.float32))
    w_qkv = np.asarray(L0_attn_w_qkv, dtype=np.float32)
    w_out = np.asarray(L0_attn_w_out, dtype=np.float32)
    qp, kpr, vp = _perm_rows()
    wqkT = np.ascontiguousarray(w_qkv[qp + kpr, :].T)
    wvT = np.ascontiguousarray(w_qkv[vp, :].T)
    woutT = np.ascontiguousarray(w_out.T)
    wrT = np.ascontiguousarray(np.asarray(L0_ffn_w_freq_real, dtype=np.float32).T)
    wiT = np.ascontiguousarray(np.asarray(L0_ffn_w_freq_imag, dtype=np.float32).T)
    cdft, sdft, sneg, rcos, rsin, tmat, mask, idf = _constants()
    biasr = np.ascontiguousarray(np.asarray(L0_ffn_bias, dtype=np.float32).reshape(1, D))
    constb = np.concatenate([np.ones((128, 128), np.float32),
                             np.zeros((128, 512), np.float32)], axis=1)

    if "nc" not in _NC_CACHE:
        _NC_CACHE["nc"] = build_nc()
    nc = _NC_CACHE["nc"]
    in_maps = []
    for c in range(NCORES):
        xs = np.ascontiguousarray(x[2 * c:2 * c + 2])
        xTs = np.ascontiguousarray(xs.transpose(0, 2, 1))
        in_maps.append({
            "x_in": xs, "xT_in": xTs, "wqkT_in": wqkT, "wvT_in": wvT,
            "woutT_in": woutT, "wrT_in": wrT, "wiT_in": wiT,
            "cdft_in": cdft, "sdft_in": sdft, "sneg_in": sneg,
            "rcos_in": rcos, "rsin_in": rsin, "tmat_in": tmat,
            "mask_in": mask, "idf_in": idf, "bias_in": biasr, "const_in": constb,
        })
    r = run_bass_kernel_spmd(nc, in_maps, core_ids=list(range(NCORES)))
    return np.concatenate([r.results[c]["y_out"] for c in range(NCORES)], axis=0)
